# version 37
# baseline (speedup 1.0000x reference)
import numpy as np
from contextlib import ExitStack

import concourse.bass as bass
import concourse.mybir as mybir
from concourse.bass_utils import run_bass_kernel_spmd

F32 = mybir.dt.float32
BF16 = mybir.dt.bfloat16
AF = mybir.ActivationFunctionType
ALU = mybir.AluOpType
AP = bass.AP

S = 2048
D = 1024
NH = 16
HD = 64
TC = 16
DC = 8
W1N = 4096
NWW = 32
TF = 512
NG = S // TF
EPS = 1e-5
NEG = -1e10


def build_program():
    nc = bass.Bass("TRN2", target_bir_lowering=False)

    x_d = nc.dram_tensor("x", [S, D], F32, kind="ExternalInput")
    wqkv_d = nc.dram_tensor("wqkv", [NH, D, 3 * HD], F32, kind="ExternalInput")
    wout_d = nc.dram_tensor("wout", [NH, HD, D], F32, kind="ExternalInput")
    w1_d = nc.dram_tensor("w1", [D, W1N], F32, kind="ExternalInput")
    b1_d = nc.dram_tensor("b1", [1, W1N], F32, kind="ExternalInput")
    w2_d = nc.dram_tensor("w2", [W1N, D], F32, kind="ExternalInput")
    b2_d = nc.dram_tensor("b2", [1, D], F32, kind="ExternalInput")
    identf_d = nc.dram_tensor("identf", [128, 128], F32, kind="ExternalInput")
    maskd_d = nc.dram_tensor("maskd", [128, 128], F32, kind="ExternalInput")
    y_d = nc.dram_tensor("y", [S, D], F32, kind="ExternalOutput")
    w1b_d = nc.dram_tensor("w1b", [D, W1N], BF16, kind="Internal")

    seq = []
    C = {"dx": 0, "dwb": 0, "dbc": 0, "dsb0": 0, "dsb1": 0,
         "dy0": 0, "dy1": 0, "dy2": 0, "dy3": 0,
         "dq0": 0, "dq1": 0, "dq2": 0, "dq3": 0,
         "dv0": 0, "dv1": 0, "dv2": 0, "dv3": 0,
         "mm": 0, "tp": 0, "ex": 0, "vv": 0}
    for _i in range(8):
        C[f"dw{_i}"] = 0
        C[f"dh{_i}"] = 0

    def vdrain():
        add("vector", lambda e: e.drain())

    def add(eng, emit, waits=(), inc=None, amt=1):
        w = [wt for wt in waits if wt[1] > 0]
        seq.append((eng, w, emit, inc, amt))
        if inc is not None:
            C[inc] += amt
            return C[inc]
        return None

    with ExitStack() as ctx:
        sb = lambda name, shape, dt: ctx.enter_context(nc.sbuf_tensor(name, shape, dt))
        x_sb = sb("x_sb", [128, TC, D], F32)
        lnf_sb = sb("lnf_sb", [128, DC, S], BF16)
        wstage = sb("wstage", [128, 8, 1024], BF16)
        identb = sb("identb_sb", [128, 128], BF16)
        maskd = sb("maskd_sb", [128, 128], BF16)
        b2r_sb = sb("b2r_sb", [1, D], BF16)
        b1c_sb = sb("b1c_sb", [128, NWW], F32)
        ones_sb = sb("ones_sb", [128, 128], BF16)
        WB_sb = sb("WB_sb", [128, D], F32)
        lntm_sb = sb("lntm_sb", [128, 4, D], BF16)
        bst_sb = sb("bst_sb", [128, 2, 12], F32)
        bag_sb = sb("bag_sb", [128, 16, 2], F32)
        rsd_sb = sb("rsd_sb", [128, 16, 2], F32)
        cst_sb = sb("cst_sb", [128, 3], F32)
        actx = ExitStack()
        asb = lambda name, shape, dt: actx.enter_context(nc.sbuf_tensor(name, shape, dt))
        v_all = asb("v_all", [128, TC, 1024], BF16)
        vaug_h = asb("vaug_h", [128, 2, TC * 128], BF16)
        qpair = asb("qpair", [128, 2, S], BF16)
        kpair = asb("kpair", [128, 2, S], BF16)
        p_sb = asb("p_sb", [128, 3, 1024], BF16)
        rec_sb = asb("rec_sb", [64, 1024], F32)
        oscr_sb = asb("oscr_sb", [128, 1024], F32)
        onorm_sb = asb("onorm_sb", [64, 2, S], BF16)
        tmp_sb = asb("tmp_sb", [128, TC, 64], BF16)

        arena = ctx.enter_context(nc.psum_tensor([128, 8, 512], F32))
        AR = 8 * 512

        def bankap(b, n=512, row0=0, rows=128):
            return AP(arena, row0 * AR + b * 512, [[AR, rows], [1, n]])

        dx_xq = []
        for xq in range(4):
            cnt = add("sync", (lambda q_: lambda e: e.dma_start(
                AP(x_sb, q_ * 4 * D, [[TC * D, 128], [D, 4], [1, D]]),
                AP(x_d, q_ * 4 * 128 * D,
                   [[D, 128], [128 * D, 4], [1, D]])))(xq),
                inc=f"dq{xq}", amt=16)
            dx_xq.append((f"dq{xq}", cnt))
        add("gpsimd", lambda e: e.dma_start(identb[:, :], identf_d.ap()),
            inc="dwb", amt=16)
        add("gpsimd", lambda e: e.dma_start(b2r_sb[:, :], b2_d.ap()),
            inc="dwb", amt=16)
        dx_consts = add("gpsimd", lambda e: e.dma_start(maskd[:, :],
                                                        maskd_d.ap()),
                        inc="dx", amt=16)
        for bcc in range(NWW):
            add("sync", (lambda c_: lambda e: e.dma_start(
                b1c_sb[:, c_:c_ + 1],
                AP(b1_d, 128 * c_, [[1, 128], [1, 1]])))(bcc),
                inc="dbc", amt=16)
        vv_ones = add("vector", lambda e: e.memset(ones_sb[:, :], 1.0), inc="vv")

        add("vector", lambda e: e.memset(cst_sb[:, 0:1], EPS))
        add("vector", lambda e: e.memset(cst_sb[:, 1:2], 0.125))
        vv_cst = add("vector", lambda e: e.memset(cst_sb[:, 2:3], -1.0), inc="vv")
        vv_vones = add("vector", lambda e: e.memset(
            AP(vaug_h, 64, [[2 * TC * 128, 128], [TC * 128, 2],
                            [128, TC], [1, 64]]), 1.0), inc="vv")

        slot_free = [("mm", 0)] * 8
        slot_ctr = [0]

        def load_slot(in_ap, out_dims):
            s = slot_ctr[0] % 8
            slot_ctr[0] += 1
            out_ap = AP(wstage, s * 1024, [[8 * 1024, 128]] + out_dims)
            cnt = add("gpsimd",
                      (lambda oa, ia: lambda e: e.dma_start(oa, ia))(out_ap, in_ap),
                      waits=_slot_waits(s, 1), inc=f"dw{s}", amt=16)
            return s, (f"dw{s}", cnt)

        def _slot_waits(s, n):
            w = []
            for i in range(n):
                v = slot_free[s + i]
                w.extend(v if isinstance(v, list) else [v])
            return w

        def load_slot_quad(in_ap, out_dims, eng="gpsimd", pre="dw"):
            while slot_ctr[0] % 4:
                slot_ctr[0] += 1
            s = slot_ctr[0] % 8
            slot_ctr[0] += 4
            out_ap = AP(wstage, s * 1024, [[8 * 1024, 128]] + out_dims)
            cnt = add(eng,
                      (lambda oa, ia: lambda e: e.dma_start(oa, ia))(out_ap, in_ap),
                      waits=_slot_waits(s, 4), inc=f"{pre}{s}", amt=16)
            return s, (f"{pre}{s}", cnt)

        def load_slot_pair(in_ap, out_dims):
            if slot_ctr[0] % 2:
                slot_ctr[0] += 1
            s = slot_ctr[0] % 8
            slot_ctr[0] += 2
            out_ap = AP(wstage, s * 1024, [[8 * 1024, 128]] + out_dims)
            w = _slot_waits(s, 2)
            cnt = add("gpsimd",
                      (lambda oa, ia: lambda e: e.dma_start(oa, ia))(out_ap, in_ap),
                      waits=w, inc=f"dw{s}", amt=16)
            return s, (f"dw{s}", cnt)

        wb_mm = None
        for kc2 in range(DC // 2):
            if kc2 == 0:
                slot_free[0] = ("dq0", 16)
            sp, dw = load_slot_pair(
                AP(wout_d, kc2 * 256 * D, [[D, 128], [128 * D, 2], [1, D]]),
                [[1024, 2], [1, 1024]])
            for r in range(2):
                kc = kc2 * 2 + r
                for h in range(2):
                    wb_mm = add("tensor",
                                (lambda s_, h_, kc_: lambda e: e.matmul(
                                    bankap(6 + h_, 512),
                                    ones_sb[:, 0:128],
                                    wstage[:, s_, h_ * 512:(h_ + 1) * 512],
                                    start=(kc_ == 0), stop=(kc_ == DC - 1),
                                    skip_group_check=True))(sp + r, h, kc),
                                waits=[dw, ("vv", vv_ones)],
                                inc="mm" if h == 1 else None)
            slot_free[sp] = ("mm", wb_mm)
            slot_free[sp + 1] = ("mm", wb_mm)
        vv_wb = add("vector", lambda e: e.tensor_copy(
            WB_sb[:, :], AP(arena, 6 * 512, [[AR, 128], [1, 1024]])),
            waits=[("mm", wb_mm)], inc="vv")

        ln_bank_free = [("vv", 0), ("vv", 0)]
        ln_tm_free = [("tp", 0)] * 4
        ex_rstd_last = [0]
        lnf_state = {"ready": ("ex", 0)}

        def ln_stats(src_col, idx, extra_v=()):
            pb = idx % 2
            vdrain()
            add("vector", (lambda sc, p_: lambda e: e.bn_stats(
                bst_sb[:, p_, 0:6], x_sb[:, sc, 0:512]))(src_col, pb),
                waits=list(extra_v) + [("ex", ex_rstd_last[0])])
            add("vector", (lambda sc, p_: lambda e: e.bn_stats(
                bst_sb[:, p_, 6:12], x_sb[:, sc, 512:1024]))(src_col, pb))
            vdrain()
            return add("vector", (lambda p_, i_: lambda e: e.bn_aggr(
                bag_sb[:, i_, :], bst_sb[:, p_, :]))(pb, idx), inc="vv")

        def ln_rstd(idx0, nt, vv_stats):
            ex_sd = add("scalar", (lambda i_, n_: lambda e: e.activation(
                AP(rsd_sb, i_ * 2, [[32, 128], [2, n_]]),
                AP(bag_sb, i_ * 2 + 1, [[32, 128], [2, n_]]),
                AF.Sqrt, bias=cst_sb[:, 0:1]))(idx0, nt),
                waits=[("vv", vv_stats)], inc="ex")
            ex_rstd_last[0] = ex_sd
            vdrain()
            return add("vector", (lambda i_, n_: lambda e: e.reciprocal(
                AP(rsd_sb, i_ * 2 + 1, [[32, 128], [2, n_]]),
                AP(rsd_sb, i_ * 2, [[32, 128], [2, n_]])))(idx0, nt),
                waits=[("ex", ex_sd)], inc="vv")

        def ln_apply_only(src_col, idx, parity, vv_rstd, buf=None, np_=4,
                          free_tbl=None):
            buf = lntm_sb if buf is None else buf
            free_tbl = ln_tm_free if free_tbl is None else free_tbl
            pb = parity % np_
            add("vector", (lambda p_: lambda e: e.drain())(pb),
                waits=[free_tbl[pb]])
            return add("vector", (lambda sc, i_, p_, b_: lambda e:
                                  e.tensor_scalar(
                b_[:, p_, :], x_sb[:, sc, :],
                bag_sb[:, i_, 0:1], rsd_sb[:, i_, 1:2],
                op0=ALU.subtract, op1=ALU.mult))(src_col, idx, pb, buf),
                waits=[("vv", vv_rstd)], inc="vv")

        def ln_transpose(dst_sb, dst_tok0, parity, vv_app, extra_tp=(),
                         buf=None, np_=4, free_tbl=None):
            buf = lntm_sb if buf is None else buf
            free_tbl = ln_tm_free if free_tbl is None else free_tbl
            pb = parity % np_
            tp_last = None
            for dc in range(DC):
                w0 = ([("vv", vv_app), ln_bank_free[pb % 2],
                       ("dwb", 32)] + list(extra_tp)) if dc == 0 else []
                tp_last = add("tensor", (lambda p_, dc_, b_: lambda e:
                                         e.matmul(
                    AP(arena, (2 * (p_ % 2)) * 512 + dc_ * 128,
                       [[AR, 128], [1, 128]]),
                    b_[:, p_, dc_ * 128:(dc_ + 1) * 128],
                    identb[:, :], start=True, stop=True,
                    skip_group_check=True))(pb, dc, buf),
                    waits=w0, inc="tp" if dc == DC - 1 else None)
            free_tbl[pb] = ("tp", tp_last)
            ex_ev = add("scalar", (lambda t0, p_, d_: lambda e: e.activation(
                d_[:, :, t0:t0 + 128],
                AP(arena, (2 * (p_ % 2)) * 512,
                   [[AR, 128], [128, 8], [1, 128]]),
                AF.Identity))(
                dst_tok0, pb, dst_sb),
                waits=[("tp", tp_last)], inc="ex")
            ln_bank_free[pb % 2] = ("ex", ex_ev)
            lnf_state["ready"] = ("ex", ex_ev)
            return ("ex", ex_ev)

        def ln_apply(src_col, idx, dst_sb, dst_tok0, parity, vv_rstd,
                     extra_tp=()):
            vv_app = ln_apply_only(src_col, idx, parity, vv_rstd)
            return ln_transpose(dst_sb, dst_tok0, parity, vv_app, extra_tp)

        wv_slots = []
        for dcc in range(DC):
            s, dw = load_slot(
                AP(wqkv_d, dcc * 128 * 192 + 128,
                   [[192, 128], [D * 192, NH], [1, HD]]),
                [[64, NH], [1, HD]])
            wv_slots.append((s, dw))
        v_bank_free = [("vv", 0), ("vv", vv_wb)]
        vv_vaug_done = ("ex", 0)
        last_v_mm = 0

        def emit_v_chunk(c, lnf_c_rdy):
            nonlocal vv_vaug_done, last_v_mm
            pb = c % 2
            b0 = 4 + 2 * pb
            mm_last = None
            for h in range(2):
                for dcc in range(DC):
                    s, dwv = wv_slots[dcc]
                    w0 = [dwv]
                    if h == 0 and dcc == 0:
                        w0 += [lnf_c_rdy, ("vv", vv_vones),
                               v_bank_free[pb]]
                    mm_last = add("tensor",
                                  (lambda s_, h_, dc_, c_, b_: lambda e: e.matmul(
                                      bankap(b_ + h_, 512),
                                      lnf_sb[:, dc_, c_ * 128:(c_ + 1) * 128],
                                      wstage[:, s_, h_ * 512:(h_ + 1) * 512],
                                      start=(dc_ == 0), stop=(dc_ == DC - 1),
                                      skip_group_check=True))(s, h, dcc, c, b0),
                                  waits=w0,
                                  inc="mm" if (h == 1 and dcc == DC - 1) else None)
            last_v_mm = mm_last
            ex_ev = add("scalar", (lambda c_, b_: lambda e: e.activation(
                v_all[:, c_, 0:1024],
                AP(arena, b_ * 512, [[AR, 128], [1, 1024]]),
                AF.Identity))(c, b0),
                waits=[("mm", mm_last)], inc="ex")
            v_bank_free[pb] = ("ex", ex_ev)
            vv_vaug_done = ("ex", ex_ev)

        vv_rstd_q = [0] * 4
        lnf_rdy_t = [None] * TC

        def ln1_stats_quarter(q):
            vst = 0
            for c in range(4 * q, 4 * q + 4):
                vst = ln_stats(c, c, extra_v=[dx_xq[q]])
            vv_rstd_q[q] = ln_rstd(4 * q, 4, vst)

        ln1_stats_quarter(0)
        for c in range(TC):
            if c % 4 == 0 and c // 4 + 1 < 4:
                ln1_stats_quarter(c // 4 + 1)
            vv_app = ln_apply_only(c, c, c, vv_rstd_q[c // 4])
            lnf_rdy_t[c] = ln_transpose(lnf_sb, c * 128, c, vv_app)
            if c >= 2:
                emit_v_chunk(c - 2, lnf_rdy_t[c - 2])
        emit_v_chunk(TC - 2, lnf_rdy_t[TC - 2])
        emit_v_chunk(TC - 1, lnf_rdy_t[TC - 1])
        vv_lnf1 = lnf_state["ready"]
        for s, _ in wv_slots:
            slot_free[s] = ("mm", last_v_mm)

        qk_bank_free = [("vv", 0), ("vv", 0)]
        s_reg_free = [vv_lnf1, vv_vaug_done, ("ex", 0)]
        REG_BASE = [1024, 3072, 0]
        o_bank_free = [vv_vaug_done, vv_vaug_done]
        vv_attn_done = 0
        ex_attn_done = 0
        vaug_free = [("mm", 0), ("mm", 0)]
        pending_permute = None
        perm_done = 0
        oscr_free = ("vv", 0)
        p_guard = [0, 0, 0]
        vv_qk_pair = {}

        def load_pair_slots(heads_):
            slots_ = []
            for hix, n in enumerate(heads_):
                sl, dwqk = load_slot(
                    AP(wqkv_d, n * D * 192,
                       [[192, 128], [128 * 192, DC], [1, 128]]),
                    [[128, DC], [1, 128]])
                slots_.append((sl, dwqk))
            return slots_

        def make_qk_chunks(pi, slots_):
            pr = pi % 2
            chunks = []
            for hix in range(2):
                sl, dwqk = slots_[hix]
                for g in range(4):
                    def chunk(extra=(), sl=sl, dwqk=dwqk, hix=hix, g=g):
                        qb2 = g % 2
                        mm_last = None
                        for dcc in range(DC):
                            w0 = ([dwqk, vv_lnf1,
                                   qk_bank_free[qb2]] + list(extra)
                                  if dcc == 0 else [])
                            mm_last = add("tensor",
                                          (lambda s_, dc_, g_, qb_: lambda e:
                                           e.matmul(
                                               bankap(qb_, 512),
                                               wstage[:, s_,
                                                      dc_ * 128:
                                                      (dc_ + 1) * 128],
                                               lnf_sb[:, dc_,
                                                      g_ * 512:(g_ + 1) * 512],
                                               start=(dc_ == 0),
                                               stop=(dc_ == DC - 1),
                                               skip_group_check=True))(
                                               sl, dcc, g, qb2),
                                          waits=w0,
                                          inc="mm" if dcc == DC - 1 else None)
                        if g == 3:
                            slot_free[sl] = ("mm", mm_last)
                        add("vector", (lambda g_, qb_, hx: lambda e:
                                       e.tensor_copy(
                            qpair[hx * 64:(hx + 1) * 64, pr,
                                  g_ * 512:(g_ + 1) * 512],
                            bankap(qb_, 512, 0, 64)))(g, qb2, hix),
                            waits=[("mm", mm_last)])
                        vvk = add("vector",
                                  (lambda g_, qb_, hx: lambda e:
                                   e.tensor_copy(
                                       kpair[hx * 64:(hx + 1) * 64, pr,
                                             g_ * 512:(g_ + 1) * 512],
                                       bankap(qb_, 512, 64, 64)))(
                                       g, qb2, hix),
                                  inc="vv")
                        qk_bank_free[qb2] = ("vv", vvk)
                        vv_qk_pair[pi] = vvk
                    chunks.append(chunk)
            return chunks

        for npair in range(NH // 2):
            heads = (2 * npair, 2 * npair + 1)
            par = npair % 2
            if npair == 0:
                for ch in make_qk_chunks(0, load_pair_slots(heads)):
                    ch()
            vv_vh = 0
            for hix, n in enumerate(heads):
                vdrain()
                vv_vh = add("vector", (lambda n_, hx: lambda e: e.tensor_copy(
                    AP(vaug_h, hx * TC * 128, [[2 * TC * 128, 128],
                                               [128, TC], [1, 64]]),
                    AP(v_all, 64 * n_, [[TC * 1024, 128],
                                        [1024, TC], [1, 64]])))(n, hix),
                    waits=[vv_vaug_done, ("vv", vv_vones),
                           vaug_free[hix]],
                    inc="vv")
            if npair + 1 < NH // 2:
                next_chunks = make_qk_chunks(
                    npair + 1,
                    load_pair_slots((2 * npair + 2, 2 * npair + 3)))
            else:
                next_chunks = []
            vv_qk_done = vv_qk_pair[npair]
            if pending_permute is not None:
                perm_done = pending_permute()
                pending_permute = None

            steps = [(qb, j) for qb in range(4) for j in range(4 * qb + 4)]
            ex_exp = {}
            mm_pv = {}
            reg_of = {}

            def emit_se(si):
                nonlocal ex_attn_done
                qb, j = steps[si]
                qlo = max(128 * j, 512 * qb)
                qhi = 512 * (qb + 1)
                w = qhi - qlo
                off = qlo - 512 * qb
                reg = reg_of[si]
                base = REG_BASE[reg]
                reg_prev = s_reg_free[reg]
                ppar_guard = ([("mm", mm_pv[si - 3])] if si >= 3 else
                              ([("mm", p_guard[si % 3])]
                               if p_guard[si % 3] else []))
                mm_sc = None
                for hix in range(2):
                    w0 = [reg_prev] if hix == 0 else []
                    if si == 0 and hix == 0:
                        w0.append(("vv", vv_qk_done))
                    mm_sc = add("tensor",
                                (lambda j_, q_, w_, b_, hx, off_, p_:
                                 lambda e:
                                 e.matmul(
                                     AP(arena, b_ + hx * 512 + off_,
                                        [[AR, 128], [1, w_]]),
                                     kpair[hx * 64:(hx + 1) * 64, p_,
                                           j_ * 128:(j_ + 1) * 128],
                                     qpair[hx * 64:(hx + 1) * 64, p_,
                                           q_:q_ + w_],
                                     start=True, stop=True,
                                     skip_group_check=True))(
                                     j, qlo, w, base, hix, off, par),
                                waits=w0, inc="mm" if hix == 1 else None)
                if qlo == 128 * j:
                    for hix in range(2):
                        mm_sc = add("tensor",
                                    (lambda b_, hx, off_: lambda e:
                                     e.matmul(
                                         AP(arena, b_ + hx * 512 + off_,
                                            [[AR, 128], [1, 128]]),
                                         maskd[:, :], identb[:, :],
                                         start=False, stop=True,
                                         skip_group_check=True))(
                                         base, hix, off),
                                    waits=([("dx", dx_consts)]
                                           if hix == 0 else []),
                                    inc="mm" if hix == 1 else None)
                ew = [("mm", mm_sc)] + ppar_guard
                last_ex = add("scalar",
                              (lambda si_, w_, b_, off_: lambda e:
                               e.activation(
                                   AP(p_sb, (si_ % 3) * 1024,
                                      [[3 * 1024, 128], [512, 2], [1, w_]]),
                                   AP(arena, b_ + off_,
                                      [[AR, 128], [512, 2], [1, w_]]),
                                   AF.Exp, scale=cst_sb[:, 1:2]))(
                                   si, w, base, off),
                              waits=ew, inc="ex")
                s_reg_free[reg] = ("ex", last_ex)
                ex_exp[si] = last_ex
                ex_attn_done = last_ex

            def emit_pv(si):
                qb, j = steps[si]
                qlo = max(128 * j, 512 * qb)
                w = 512 * (qb + 1) - qlo
                off = qlo - 512 * qb
                jstop = 4 * qb + 3
                last = None
                for hix in range(2):
                    w0 = [("ex", ex_exp[si])] if hix == 0 else []
                    if j == 0 and hix == 0:
                        w0.append(o_bank_free[0])
                        w0.append(o_bank_free[1])
                        w0.append(("vv", vv_vh))
                    last = add("tensor",
                               (lambda j_, w_, hx, st_, si_, off_: lambda e:
                                e.matmul(
                                    AP(arena, (4 + hx) * 512 + off_,
                                       [[AR, 128], [1, w_]]),
                                    vaug_h[:, hx, j_ * 128:(j_ + 1) * 128],
                                    AP(p_sb, (si_ % 3) * 1024 + hx * 512,
                                       [[3 * 1024, 128], [1, w_]]),
                                    start=(j_ == 0), stop=st_,
                                    skip_group_check=True))(
                                    j, w, hix, j == jstop, si, off),
                               waits=w0, inc="mm" if hix == 1 else None)
                mm_pv[si] = last
                return last

            def emit_epilogue(qb, mmw):
                for hix in range(2):
                    vdrain()
                    vv_oc = add("vector", (lambda hx: lambda e: e.tensor_copy(
                        AP(oscr_sb, hx * 512, [[1024, 128], [1, 512]]),
                        AP(arena, (4 + hx) * 512, [[AR, 128], [1, 512]])))(hix),
                        waits=[("mm", mmw), oscr_free],
                        inc="vv")
                    o_bank_free[hix] = ("vv", vv_oc)
                vv_oc_last = o_bank_free[1]

                def epi_b(qb=qb, vv_oc_last=vv_oc_last, pd=perm_done):
                    nonlocal oscr_free
                    add("scalar", lambda e: e.activation(
                        rec_sb[0:64, :],
                        AP(oscr_sb, 64 * 1024, [[1024, 64], [1, 1024]]),
                        AF.Ln),
                        waits=[vv_oc_last, ("vv", pd)])
                    add("scalar", lambda e: e.drain())
                    ex_rec = add("scalar", lambda e: e.activation(
                        rec_sb[0:64, :], rec_sb[0:64, :],
                        AF.Exp, scale=cst_sb[0:64, 2:3]),
                        waits=[("vv", vv_cst)], inc="ex")
                    vdrain()
                    vv_on = None
                    for hix in range(2):
                        vv_on = add("vector", (lambda qb_, hx: lambda e:
                                    e.tensor_mul(
                                        AP(onorm_sb, hx * S + qb_ * 512,
                                           [[2 * S, 64], [1, 512]]),
                                        AP(oscr_sb, hx * 512,
                                           [[1024, 64], [1, 512]]),
                                        rec_sb[0:64,
                                               hx * 512:(hx + 1) * 512]))(
                                        qb, hix),
                                    waits=[("ex", ex_rec)],
                                    inc="vv")
                    oscr_free = ("vv", vv_on)
                return epi_b

            s_reg_free[2] = ("vv", vv_qk_pair[npair])
            rot = []
            ck_slots = {8, 17, 26, 35} if next_chunks else set()
            si_ctr = 0
            kk = 0
            rc = 0
            while si_ctr < len(steps):
                if rc in ck_slots and kk < len(next_chunks):
                    rot.append(("ck", kk))
                    kk += 2
                else:
                    reg_of[si_ctr] = rc % 3
                    rot.append(("se", si_ctr))
                    si_ctr += 1
                rc += 1
            r = 0
            nse = 0
            npv = 0
            epi_b_pend = None
            epi_b_due = 0
            while npv < len(steps):
                if r < len(rot) and nse - npv < 3:
                    kind, v = rot[r]
                    if kind == "se":
                        emit_se(v)
                        nse += 1
                    else:
                        for ck in (v, v + 1):
                            next_chunks[ck](extra=[s_reg_free[2]])
                        s_reg_free[2] = ("vv", vv_qk_pair[npair + 1])
                    r += 1
                else:
                    emit_pv(npv)
                    qb, j = steps[npv]
                    if j == 4 * qb + 3:
                        epi_b_pend = emit_epilogue(qb, mm_pv[npv])
                        epi_b_due = npv + 3
                    npv += 1
                    if epi_b_pend is not None and npv >= epi_b_due:
                        epi_b_pend()
                        epi_b_pend = None
            if epi_b_pend is not None:
                epi_b_pend()
                epi_b_pend = None
            for si in range(len(steps) - 3, len(steps)):
                p_guard[si % 3] = mm_pv[si]
            vaug_free[0] = ("mm", C["mm"])
            vaug_free[1] = ("mm", C["mm"])

            def make_permutes(heads_):
                def emit():
                    nonlocal vv_attn_done
                    for hix, n_ in enumerate(heads_):
                        for h2 in range(2):
                            vdrain()
                            add("vector",
                                (lambda nn, h2_, hx: lambda e: e.tensor_mul(
                                    tmp_sb[h2_ * 64:(h2_ + 1) * 64, :, :],
                                    AP(onorm_sb, hx * S + h2_,
                                       [[2 * S, 64], [2, TC], [32, 64]]),
                                    AP(WB_sb, 64 * nn,
                                       [[D, 64], [0, TC], [1, 64]])))(
                                    n_, h2, hix),
                                waits=[("vv", vv_wb)])
                            vdrain()
                            vv_attn_done = add(
                                "vector",
                                (lambda nn, h2_: lambda e: e.tensor_add(
                                    x_sb[h2_ * 64:(h2_ + 1) * 64, :,
                                         64 * nn:64 * nn + 64],
                                    x_sb[h2_ * 64:(h2_ + 1) * 64, :,
                                         64 * nn:64 * nn + 64],
                                    tmp_sb[h2_ * 64:(h2_ + 1) * 64, :, :]))(
                                    n_, h2), inc="vv")
                    return vv_attn_done
                return emit
            pending_permute = make_permutes(heads)

        if pending_permute is not None:
            perm_done = pending_permute()
            pending_permute = None

        actx.close()
        h1_sb = sb("h1_sb", [128, NWW, TF], BF16)
        y_sb = sb("y_sb", [128, 4, D], F32)
        w2stage = sb("w2stage", [128, 8, 1024], BF16)

        f1_bank_free = [("vv", 0)] * 4 + [o_bank_free[0], o_bank_free[1]]
        ln2_rstd = [0, 0]
        ln2_apps = [None, None]
        ffn2_evac4 = ("vv", 0)
        w2q_free = [("mm", 0)] * 4
        out_bank_free = []
        ysb_free = [("dx", 0)] * 4
        dx_final = 0
        for g in range(NG):
            if g == 0:
                vst = 0
                for t in range(4):
                    vst = ln_stats(g * 4 + t, (g % 2) * 4 + t,
                                   extra_v=[("vv", vv_attn_done)])
                ln2_rstd[g % 2] = ln_rstd((g % 2) * 4, 4, vst)
                ln2_apps[g % 2] = [
                    ln_apply_only(g * 4 + t, (g % 2) * 4 + t, t,
                                  ln2_rstd[g % 2],
)
                    for t in range(4)]
            for t in range(4):
                c = g * 4 + t
                extra_tp = ([("ex", ex_attn_done)] if (g == 0 and t <= 1)
                            else ([ffn2_evac4] if (g > 0 and t <= 1) else []))
                ln_transpose(lnf_sb, t * 128, t, ln2_apps[g % 2][t],
                             extra_tp=extra_tp)
            vv_ln2 = lnf_state["ready"]
            for ww in range(NWW):
                if ww % 4 == 0:
                    w1ap = [[W1N, 128], [128 * W1N, DC], [1, 512]]
                    if g == 0:
                        sp, dwp = load_slot_quad(
                            AP(w1_d, 128 * ww, w1ap),
                            [[512, DC], [1, 512]])
                        dsb_c = add("scalar",
                                    (lambda s_, ww_, a_: lambda e:
                                     e.dma_start(
                                         AP(w1b_d, 128 * ww_, a_),
                                         AP(wstage, s_ * 1024,
                                            [[8 * 1024, 128], [512, DC],
                                             [1, 512]])))(sp, ww, w1ap),
                                    waits=[dwp],
                                    inc=f"dsb{(ww // 4) % 2}", amt=16)
                        wb_pend = (f"dsb{(ww // 4) % 2}", dsb_c)
                    else:
                        sp, dwp = load_slot_quad(
                            AP(w1b_d, 128 * ww, w1ap),
                            [[512, DC], [1, 512]],
                            eng="scalar", pre="dh")
                        wb_pend = None
                s, dww = sp, dwp
                wpar = ww % 4
                fb = ww % 6
                mm_f1 = None
                for dcc in range(DC):
                    w0 = ([dww, vv_ln2, f1_bank_free[fb]]
                          if dcc == 0 else [])
                    mm_f1 = add("tensor",
                                (lambda s_, dc_, fb_, wp_: lambda e: e.matmul(
                                    bankap(fb_, TF),
                                    AP(wstage, s_ * 1024 + dc_ * 512
                                       + wp_ * 128,
                                       [[8 * 1024, 128], [1, 128]]),
                                    lnf_sb[:, dc_, 0:TF],
                                    start=(dc_ == 0), stop=(dc_ == DC - 1),
                                    skip_group_check=True))(s, dcc, fb, wpar),
                                waits=w0,
                                inc="mm" if dcc == DC - 1 else None)
                if ww % 4 == 3:
                    for i in range(4):
                        slot_free[s + i] = ([("mm", mm_f1), wb_pend]
                                            if wb_pend else ("mm", mm_f1))
                vv_r = add("vector", (lambda ww_, fb_: lambda e: e.tensor_scalar(
                    h1_sb[:, ww_, :], bankap(fb_, TF),
                    b1c_sb[:, ww_:ww_ + 1], 0.0,
                    op0=ALU.add, op1=ALU.max))(ww, fb),
                    waits=[("mm", mm_f1)] +
                          ([("dbc", 16 * NWW)] if ww == 0 else []), inc="vv")
                f1_bank_free[fb] = ("vv", vv_r)
            vv_h1 = C["vv"]
            for ww in range(NWW):
                if ww % 2 == 0:
                    wq = (ww // 2) % 4
                    sp2 = wq * 2
                    add("gpsimd",
                        (lambda sp_, ww_: lambda e: e.dma_start(
                            AP(w2stage, sp_ * 1024,
                               [[8 * 1024, 128], [1024, 2], [1, 1024]]),
                            AP(w2_d, ww_ * 128 * D,
                               [[D, 128], [128 * D, 2], [1, D]])))(sp2, ww),
                        waits=[w2q_free[wq]], inc=f"dv{wq}", amt=16)
                    dwp = (f"dv{wq}", C[f"dv{wq}"])
                s, dww = sp2 + (ww % 2), dwp
                mm_t = None
                for t in range(8):
                    c2, dh = t // 2, t % 2
                    w0 = []
                    if t == 0:
                        w0 = [dww]
                        if ww == 0:
                            w0 += [("vv", vv_h1)] + out_bank_free
                    elif t == 7:
                        w0 = [(dww[0], dww[1], True)]
                    mm_t = add("tensor",
                               (lambda s_, ww_, c2_, dh_, t_: lambda e: e.matmul(
                                   bankap(t_, 512),
                                   h1_sb[:, ww_, 128 * c2_:128 * (c2_ + 1)],
                                   w2stage[:, s_, 512 * dh_:512 * (dh_ + 1)],
                                   start=(ww_ == 0), stop=False,
                                   skip_group_check=True))(s, ww, c2, dh, t),
                               waits=w0,
                               inc="mm" if t == 7 else None)
                if ww % 2 == 1:
                    w2q_free[(ww // 2) % 4] = ("mm", mm_t)
            mm_aug = [None] * 8
            for t in range(8):
                dh = t % 2
                mm_aug[t] = add("tensor",
                                (lambda dh_, t_: lambda e: e.matmul(
                                    bankap(t_, 512),
                                    ones_sb[0:1, 0:128],
                                    b2r_sb[0:1, 512 * dh_:512 * (dh_ + 1)],
                                    start=False, stop=True,
                                    skip_group_check=True))(dh, t),
                                waits=[("dwb", 32)], inc="mm")
            if g + 1 < NG:
                vst = 0
                for t in range(4):
                    vst = ln_stats((g + 1) * 4 + t, ((g + 1) % 2) * 4 + t)
                ln2_rstd[(g + 1) % 2] = ln_rstd(((g + 1) % 2) * 4, 4, vst)
                ln2_apps[(g + 1) % 2] = [
                    ln_apply_only((g + 1) * 4 + t, ((g + 1) % 2) * 4 + t,
                                  t, ln2_rstd[(g + 1) % 2],
)
                    for t in range(4)]
            out_bank_free = []
            for t in range(8):
                c2, dh = t // 2, t % 2
                cglob = g * 4 + c2
                yp = cglob % 4
                vv_y = add("vector",
                           (lambda t_, dh_, cg, yp_: lambda e: e.tensor_add(
                               y_sb[:, yp_, 512 * dh_:512 * (dh_ + 1)],
                               bankap(t_, 512),
                               x_sb[:, cg, 512 * dh_:512 * (dh_ + 1)]))(
                               t, dh, cglob, yp),
                           waits=[("mm", mm_aug[t])] +
                                 ([ysb_free[yp]] if dh == 0 else []),
                           inc="vv")
                out_bank_free.append(("vv", vv_y))
                if t == 3:
                    ffn2_evac4 = ("vv", vv_y)
                if dh == 1:
                    dyc = add("sync",
                              (lambda cg, yp_: lambda e: e.dma_start(
                                  AP(y_d, cg * 128 * D, [[D, 128], [1, D]]),
                                  y_sb[:, yp_, :]))(cglob, yp),
                              waits=[("vv", vv_y)], inc=f"dy{yp}", amt=16)
                    ysb_free[yp] = (f"dy{yp}", dyc)
            for b in range(6):
                f1_bank_free[b] = out_bank_free[b]

        add("sync", None, waits=[("dx", C["dx"]), ("dbc", C["dbc"]),
                                 ("dsb0", C["dsb0"]), ("dsb1", C["dsb1"])] +
                          [(f"dy{i}", C[f"dy{i}"]) for i in range(4)] +
                          [(f"dq{i}", C[f"dq{i}"]) for i in range(4)] +
                          [(f"dh{i}", C[f"dh{i}"]) for i in range(8)])
        add("gpsimd", None,
            waits=[("dwb", C["dwb"])] +
                  [(f"dv{i}", C[f"dv{i}"]) for i in range(4)] +
                  [(f"dw{i}", C[f"dw{i}"]) for i in range(8)])

        sems = {name: ctx.enter_context(nc.semaphore(name)) for name in C}

        with nc.Block() as block:
            def emit_for(eng_name):
                def body(eng_obj):
                    last_wait = {}
                    for (e, waits, emit, inc, amt) in seq:
                        if e != eng_name:
                            continue
                        for wt in waits:
                            s_, v = wt[0], wt[1]
                            force = len(wt) > 2 and wt[2]
                            if v > 0 and (force or last_wait.get(s_, -1) < v):
                                eng_obj.wait_ge(sems[s_], v)
                                last_wait[s_] = max(last_wait.get(s_, -1), v)
                        if emit is None:
                            continue
                        inst = emit(eng_obj)
                        if inc is not None:
                            inst.then_inc(sems[inc], amt)
                return body

            block.sync(emit_for("sync"))
            block.gpsimd(emit_for("gpsimd"))
            block.tensor(emit_for("tensor"))
            block.vector(emit_for("vector"))
            block.scalar(emit_for("scalar"))

    return nc


def kernel(L0_attn_x, L0_attn_W_qkv, L0_attn_W_out, L0_ffn_w1, L0_ffn_b1,
           L0_ffn_w2, L0_ffn_b2):
    x = np.ascontiguousarray(np.asarray(L0_attn_x, dtype=np.float32))
    wqkv = np.ascontiguousarray(np.asarray(L0_attn_W_qkv, dtype=np.float32))
    wout = np.ascontiguousarray(np.asarray(L0_attn_W_out, dtype=np.float32))
    w1 = np.ascontiguousarray(np.asarray(L0_ffn_w1, dtype=np.float32))
    b1 = np.ascontiguousarray(np.asarray(L0_ffn_b1, dtype=np.float32)).reshape(1, -1)
    w2 = np.ascontiguousarray(np.asarray(L0_ffn_w2, dtype=np.float32))
    b2 = np.ascontiguousarray(np.asarray(L0_ffn_b2, dtype=np.float32)).reshape(1, -1)

    identf = np.eye(128, dtype=np.float32)
    maskd = np.triu(np.full((128, 128), NEG, dtype=np.float32), 1)

    nc = build_program()
    in_maps = []
    for b in range(8):
        in_maps.append({
            "x": np.ascontiguousarray(x[b]),
            "wqkv": wqkv, "wout": wout, "w1": w1, "b1": b1,
            "w2": w2, "b2": b2, "identf": identf, "maskd": maskd,
        })
    res = run_bass_kernel_spmd(nc, in_maps, core_ids=list(range(8)))
    out = np.stack([np.asarray(r["y"]) for r in res.results], axis=0)
    return out


# version 38 (final)
# speedup vs baseline: 1.0228x; 1.0228x over previous
import numpy as np
from contextlib import ExitStack

import concourse.bass as bass
import concourse.mybir as mybir
from concourse.bass_utils import run_bass_kernel_spmd

F32 = mybir.dt.float32
BF16 = mybir.dt.bfloat16
AF = mybir.ActivationFunctionType
ALU = mybir.AluOpType
AP = bass.AP

S = 2048
D = 1024
NH = 16
HD = 64
TC = 16
DC = 8
W1N = 4096
NWW = 32
TF = 512
NG = S // TF
EPS = 1e-5
NEG = -1e10


def build_program():
    nc = bass.Bass("TRN2", target_bir_lowering=False)

    x_d = nc.dram_tensor("x", [S, D], F32, kind="ExternalInput")
    wqkv_d = nc.dram_tensor("wqkv", [NH, D, 3 * HD], F32, kind="ExternalInput")
    wout_d = nc.dram_tensor("wout", [NH, HD, D], F32, kind="ExternalInput")
    w1_d = nc.dram_tensor("w1", [D, W1N], F32, kind="ExternalInput")
    b1_d = nc.dram_tensor("b1", [1, W1N], F32, kind="ExternalInput")
    w2_d = nc.dram_tensor("w2", [W1N, D], F32, kind="ExternalInput")
    b2_d = nc.dram_tensor("b2", [1, D], F32, kind="ExternalInput")
    identf_d = nc.dram_tensor("identf", [128, 128], F32, kind="ExternalInput")
    maskd_d = nc.dram_tensor("maskd", [128, 128], F32, kind="ExternalInput")
    y_d = nc.dram_tensor("y", [S, D], F32, kind="ExternalOutput")
    w1b_d = nc.dram_tensor("w1b", [D, W1N], BF16, kind="Internal")

    seq = []
    C = {"dx": 0, "dwb": 0, "dbc": 0, "dsb0": 0, "dsb1": 0,
         "dy0": 0, "dy1": 0, "dy2": 0, "dy3": 0,
         "dq0": 0, "dq1": 0, "dq2": 0, "dq3": 0,
         "dv0": 0, "dv1": 0, "dv2": 0, "dv3": 0,
         "mm": 0, "tp": 0, "ex": 0, "vv": 0}
    for _i in range(8):
        C[f"dw{_i}"] = 0
        C[f"dh{_i}"] = 0

    def vdrain():
        add("vector", lambda e: e.drain())

    def add(eng, emit, waits=(), inc=None, amt=1):
        w = [wt for wt in waits if wt[1] > 0]
        seq.append((eng, w, emit, inc, amt))
        if inc is not None:
            C[inc] += amt
            return C[inc]
        return None

    with ExitStack() as ctx:
        sb = lambda name, shape, dt: ctx.enter_context(nc.sbuf_tensor(name, shape, dt))
        x_sb = sb("x_sb", [128, TC, D], F32)
        lnf_sb = sb("lnf_sb", [128, DC, S], BF16)
        wstage = sb("wstage", [128, 8, 1024], BF16)
        identb = sb("identb_sb", [128, 128], BF16)
        maskd = sb("maskd_sb", [128, 128], BF16)
        b2r_sb = sb("b2r_sb", [1, D], BF16)
        b1c_sb = sb("b1c_sb", [128, NWW], F32)
        ones_sb = sb("ones_sb", [128, 128], BF16)
        WB_sb = sb("WB_sb", [128, D], F32)
        lntm_sb = sb("lntm_sb", [128, 4, D], BF16)
        bst_sb = sb("bst_sb", [128, 2, 12], F32)
        bag_sb = sb("bag_sb", [128, 16, 2], F32)
        rsd_sb = sb("rsd_sb", [128, 16, 2], F32)
        cst_sb = sb("cst_sb", [128, 3], F32)
        actx = ExitStack()
        asb = lambda name, shape, dt: actx.enter_context(nc.sbuf_tensor(name, shape, dt))
        v_all = asb("v_all", [128, TC, 1024], BF16)
        vaug_h = asb("vaug_h", [128, 2, TC * 128], BF16)
        qpair = asb("qpair", [128, 2, S], BF16)
        kpair = asb("kpair", [128, 2, S], BF16)
        p_sb = asb("p_sb", [128, 3, 1024], BF16)
        rec_sb = asb("rec_sb", [64, 1024], F32)
        oscr_sb = asb("oscr_sb", [128, 1024], F32)
        onorm_sb = asb("onorm_sb", [64, 2, S], BF16)
        tmp_sb = asb("tmp_sb", [128, TC, 64], BF16)

        arena = ctx.enter_context(nc.psum_tensor([128, 8, 512], F32))
        AR = 8 * 512

        def bankap(b, n=512, row0=0, rows=128):
            return AP(arena, row0 * AR + b * 512, [[AR, rows], [1, n]])

        dx_xq = []
        for xq in range(4):
            cnt = add("sync", (lambda q_: lambda e: e.dma_start(
                AP(x_sb, q_ * 4 * D, [[TC * D, 128], [D, 4], [1, D]]),
                AP(x_d, q_ * 4 * 128 * D,
                   [[D, 128], [128 * D, 4], [1, D]])))(xq),
                waits=([(f"dq{xq - 1}", 16)] if xq > 0 else []),
                inc=f"dq{xq}", amt=16)
            dx_xq.append((f"dq{xq}", cnt))
        add("gpsimd", lambda e: e.dma_start(identb[:, :], identf_d.ap()),
            inc="dwb", amt=16)
        add("gpsimd", lambda e: e.dma_start(b2r_sb[:, :], b2_d.ap()),
            inc="dwb", amt=16)
        dx_consts = add("gpsimd", lambda e: e.dma_start(maskd[:, :],
                                                        maskd_d.ap()),
                        inc="dx", amt=16)
        for bcc in range(NWW):
            add("sync", (lambda c_: lambda e: e.dma_start(
                b1c_sb[:, c_:c_ + 1],
                AP(b1_d, 128 * c_, [[1, 128], [1, 1]])))(bcc),
                inc="dbc", amt=16)
        vv_ones = add("vector", lambda e: e.memset(ones_sb[:, :], 1.0), inc="vv")

        add("vector", lambda e: e.memset(cst_sb[:, 0:1], EPS))
        add("vector", lambda e: e.memset(cst_sb[:, 1:2], 0.125))
        vv_cst = add("vector", lambda e: e.memset(cst_sb[:, 2:3], -1.0), inc="vv")
        vv_vones = add("vector", lambda e: e.memset(
            AP(vaug_h, 64, [[2 * TC * 128, 128], [TC * 128, 2],
                            [128, TC], [1, 64]]), 1.0), inc="vv")

        slot_free = [("mm", 0)] * 8
        slot_ctr = [0]

        def load_slot(in_ap, out_dims):
            s = slot_ctr[0] % 8
            slot_ctr[0] += 1
            out_ap = AP(wstage, s * 1024, [[8 * 1024, 128]] + out_dims)
            cnt = add("gpsimd",
                      (lambda oa, ia: lambda e: e.dma_start(oa, ia))(out_ap, in_ap),
                      waits=_slot_waits(s, 1), inc=f"dw{s}", amt=16)
            return s, (f"dw{s}", cnt)

        def _slot_waits(s, n):
            w = []
            for i in range(n):
                v = slot_free[s + i]
                w.extend(v if isinstance(v, list) else [v])
            return w

        def load_slot_quad(in_ap, out_dims, eng="gpsimd", pre="dw"):
            while slot_ctr[0] % 4:
                slot_ctr[0] += 1
            s = slot_ctr[0] % 8
            slot_ctr[0] += 4
            out_ap = AP(wstage, s * 1024, [[8 * 1024, 128]] + out_dims)
            cnt = add(eng,
                      (lambda oa, ia: lambda e: e.dma_start(oa, ia))(out_ap, in_ap),
                      waits=_slot_waits(s, 4), inc=f"{pre}{s}", amt=16)
            return s, (f"{pre}{s}", cnt)

        def load_slot_pair(in_ap, out_dims):
            if slot_ctr[0] % 2:
                slot_ctr[0] += 1
            s = slot_ctr[0] % 8
            slot_ctr[0] += 2
            out_ap = AP(wstage, s * 1024, [[8 * 1024, 128]] + out_dims)
            w = _slot_waits(s, 2)
            cnt = add("gpsimd",
                      (lambda oa, ia: lambda e: e.dma_start(oa, ia))(out_ap, in_ap),
                      waits=w, inc=f"dw{s}", amt=16)
            return s, (f"dw{s}", cnt)

        wb_mm = None
        for kc2 in range(DC // 2):
            if kc2 == 0:
                slot_free[0] = ("dq0", 16)
            sp, dw = load_slot_pair(
                AP(wout_d, kc2 * 256 * D, [[D, 128], [128 * D, 2], [1, D]]),
                [[1024, 2], [1, 1024]])
            for r in range(2):
                kc = kc2 * 2 + r
                for h in range(2):
                    wb_mm = add("tensor",
                                (lambda s_, h_, kc_: lambda e: e.matmul(
                                    bankap(6 + h_, 512),
                                    ones_sb[:, 0:128],
                                    wstage[:, s_, h_ * 512:(h_ + 1) * 512],
                                    start=(kc_ == 0), stop=(kc_ == DC - 1),
                                    skip_group_check=True))(sp + r, h, kc),
                                waits=[dw, ("vv", vv_ones)],
                                inc="mm" if h == 1 else None)
            slot_free[sp] = ("mm", wb_mm)
            slot_free[sp + 1] = ("mm", wb_mm)
        vv_wb = add("vector", lambda e: e.tensor_copy(
            WB_sb[:, :], AP(arena, 6 * 512, [[AR, 128], [1, 1024]])),
            waits=[("mm", wb_mm)], inc="vv")

        ln_bank_free = [("vv", 0), ("vv", 0)]
        ln_tm_free = [("tp", 0)] * 4
        ex_rstd_last = [0]
        lnf_state = {"ready": ("ex", 0)}

        def ln_stats(src_col, idx, extra_v=()):
            pb = idx % 2
            vdrain()
            add("vector", (lambda sc, p_: lambda e: e.bn_stats(
                bst_sb[:, p_, 0:6], x_sb[:, sc, 0:512]))(src_col, pb),
                waits=list(extra_v) + [("ex", ex_rstd_last[0])])
            add("vector", (lambda sc, p_: lambda e: e.bn_stats(
                bst_sb[:, p_, 6:12], x_sb[:, sc, 512:1024]))(src_col, pb))
            vdrain()
            return add("vector", (lambda p_, i_: lambda e: e.bn_aggr(
                bag_sb[:, i_, :], bst_sb[:, p_, :]))(pb, idx), inc="vv")

        def ln_rstd(idx0, nt, vv_stats):
            ex_sd = add("scalar", (lambda i_, n_: lambda e: e.activation(
                AP(rsd_sb, i_ * 2, [[32, 128], [2, n_]]),
                AP(bag_sb, i_ * 2 + 1, [[32, 128], [2, n_]]),
                AF.Sqrt, bias=cst_sb[:, 0:1]))(idx0, nt),
                waits=[("vv", vv_stats)], inc="ex")
            ex_rstd_last[0] = ex_sd
            vdrain()
            return add("vector", (lambda i_, n_: lambda e: e.reciprocal(
                AP(rsd_sb, i_ * 2 + 1, [[32, 128], [2, n_]]),
                AP(rsd_sb, i_ * 2, [[32, 128], [2, n_]])))(idx0, nt),
                waits=[("ex", ex_sd)], inc="vv")

        def ln_apply_only(src_col, idx, parity, vv_rstd, buf=None, np_=4,
                          free_tbl=None):
            buf = lntm_sb if buf is None else buf
            free_tbl = ln_tm_free if free_tbl is None else free_tbl
            pb = parity % np_
            add("vector", (lambda p_: lambda e: e.drain())(pb),
                waits=[free_tbl[pb]])
            return add("vector", (lambda sc, i_, p_, b_: lambda e:
                                  e.tensor_scalar(
                b_[:, p_, :], x_sb[:, sc, :],
                bag_sb[:, i_, 0:1], rsd_sb[:, i_, 1:2],
                op0=ALU.subtract, op1=ALU.mult))(src_col, idx, pb, buf),
                waits=[("vv", vv_rstd)], inc="vv")

        def ln_transpose(dst_sb, dst_tok0, parity, vv_app, extra_tp=(),
                         buf=None, np_=4, free_tbl=None):
            buf = lntm_sb if buf is None else buf
            free_tbl = ln_tm_free if free_tbl is None else free_tbl
            pb = parity % np_
            tp_last = None
            for dc in range(DC):
                w0 = ([("vv", vv_app), ln_bank_free[pb % 2],
                       ("dwb", 32)] + list(extra_tp)) if dc == 0 else []
                tp_last = add("tensor", (lambda p_, dc_, b_: lambda e:
                                         e.matmul(
                    AP(arena, (2 * (p_ % 2)) * 512 + dc_ * 128,
                       [[AR, 128], [1, 128]]),
                    b_[:, p_, dc_ * 128:(dc_ + 1) * 128],
                    identb[:, :], start=True, stop=True,
                    skip_group_check=True))(pb, dc, buf),
                    waits=w0, inc="tp" if dc == DC - 1 else None)
            free_tbl[pb] = ("tp", tp_last)
            ex_ev = add("scalar", (lambda t0, p_, d_: lambda e: e.activation(
                d_[:, :, t0:t0 + 128],
                AP(arena, (2 * (p_ % 2)) * 512,
                   [[AR, 128], [128, 8], [1, 128]]),
                AF.Identity))(
                dst_tok0, pb, dst_sb),
                waits=[("tp", tp_last)], inc="ex")
            ln_bank_free[pb % 2] = ("ex", ex_ev)
            lnf_state["ready"] = ("ex", ex_ev)
            return ("ex", ex_ev)

        def ln_apply(src_col, idx, dst_sb, dst_tok0, parity, vv_rstd,
                     extra_tp=()):
            vv_app = ln_apply_only(src_col, idx, parity, vv_rstd)
            return ln_transpose(dst_sb, dst_tok0, parity, vv_app, extra_tp)

        wv_slots = []
        for dcc in range(DC):
            s, dw = load_slot(
                AP(wqkv_d, dcc * 128 * 192 + 128,
                   [[192, 128], [D * 192, NH], [1, HD]]),
                [[64, NH], [1, HD]])
            wv_slots.append((s, dw))
        v_bank_free = [("vv", 0), ("vv", vv_wb)]
        vv_vaug_done = ("ex", 0)
        last_v_mm = 0

        def emit_v_chunk(c, lnf_c_rdy):
            nonlocal vv_vaug_done, last_v_mm
            pb = c % 2
            b0 = 4 + 2 * pb
            mm_last = None
            for h in range(2):
                for dcc in range(DC):
                    s, dwv = wv_slots[dcc]
                    w0 = [dwv]
                    if h == 0 and dcc == 0:
                        w0 += [lnf_c_rdy, ("vv", vv_vones),
                               v_bank_free[pb]]
                    mm_last = add("tensor",
                                  (lambda s_, h_, dc_, c_, b_: lambda e: e.matmul(
                                      bankap(b_ + h_, 512),
                                      lnf_sb[:, dc_, c_ * 128:(c_ + 1) * 128],
                                      wstage[:, s_, h_ * 512:(h_ + 1) * 512],
                                      start=(dc_ == 0), stop=(dc_ == DC - 1),
                                      skip_group_check=True))(s, h, dcc, c, b0),
                                  waits=w0,
                                  inc="mm" if (h == 1 and dcc == DC - 1) else None)
            last_v_mm = mm_last
            ex_ev = add("scalar", (lambda c_, b_: lambda e: e.activation(
                v_all[:, c_, 0:1024],
                AP(arena, b_ * 512, [[AR, 128], [1, 1024]]),
                AF.Identity))(c, b0),
                waits=[("mm", mm_last)], inc="ex")
            v_bank_free[pb] = ("ex", ex_ev)
            vv_vaug_done = ("ex", ex_ev)

        vv_rstd_q = [0] * 4
        lnf_rdy_t = [None] * TC

        def ln1_stats_quarter(q):
            vst = 0
            for c in range(4 * q, 4 * q + 4):
                vst = ln_stats(c, c, extra_v=[dx_xq[q]])
            vv_rstd_q[q] = ln_rstd(4 * q, 4, vst)

        ln1_stats_quarter(0)
        for c in range(TC):
            if c % 4 == 0 and c // 4 + 1 < 4:
                ln1_stats_quarter(c // 4 + 1)
            vv_app = ln_apply_only(c, c, c, vv_rstd_q[c // 4])
            lnf_rdy_t[c] = ln_transpose(lnf_sb, c * 128, c, vv_app)
            if c >= 2:
                emit_v_chunk(c - 2, lnf_rdy_t[c - 2])
        emit_v_chunk(TC - 2, lnf_rdy_t[TC - 2])
        emit_v_chunk(TC - 1, lnf_rdy_t[TC - 1])
        vv_lnf1 = lnf_state["ready"]
        for s, _ in wv_slots:
            slot_free[s] = ("mm", last_v_mm)

        qk_bank_free = [("vv", 0), ("vv", 0)]
        s_reg_free = [vv_lnf1, vv_vaug_done, ("ex", 0)]
        REG_BASE = [1024, 3072, 0]
        o_bank_free = [vv_vaug_done, vv_vaug_done]
        vv_attn_done = 0
        ex_attn_done = 0
        vaug_free = [("mm", 0), ("mm", 0)]
        pending_permute = None
        perm_done = 0
        oscr_free = ("vv", 0)
        p_guard = [0, 0, 0]
        vv_qk_pair = {}

        def load_pair_slots(heads_):
            slots_ = []
            for hix, n in enumerate(heads_):
                sl, dwqk = load_slot(
                    AP(wqkv_d, n * D * 192,
                       [[192, 128], [128 * 192, DC], [1, 128]]),
                    [[128, DC], [1, 128]])
                slots_.append((sl, dwqk))
            return slots_

        def make_qk_chunks(pi, slots_):
            pr = pi % 2
            chunks = []
            for hix in range(2):
                sl, dwqk = slots_[hix]
                for g in range(4):
                    def chunk(extra=(), sl=sl, dwqk=dwqk, hix=hix, g=g):
                        qb2 = g % 2
                        mm_last = None
                        for dcc in range(DC):
                            w0 = ([dwqk, vv_lnf1,
                                   qk_bank_free[qb2]] + list(extra)
                                  if dcc == 0 else [])
                            mm_last = add("tensor",
                                          (lambda s_, dc_, g_, qb_: lambda e:
                                           e.matmul(
                                               bankap(qb_, 512),
                                               wstage[:, s_,
                                                      dc_ * 128:
                                                      (dc_ + 1) * 128],
                                               lnf_sb[:, dc_,
                                                      g_ * 512:(g_ + 1) * 512],
                                               start=(dc_ == 0),
                                               stop=(dc_ == DC - 1),
                                               skip_group_check=True))(
                                               sl, dcc, g, qb2),
                                          waits=w0,
                                          inc="mm" if dcc == DC - 1 else None)
                        if g == 3:
                            slot_free[sl] = ("mm", mm_last)
                        add("vector", (lambda g_, qb_, hx: lambda e:
                                       e.tensor_copy(
                            qpair[hx * 64:(hx + 1) * 64, pr,
                                  g_ * 512:(g_ + 1) * 512],
                            bankap(qb_, 512, 0, 64)))(g, qb2, hix),
                            waits=[("mm", mm_last)])
                        vvk = add("vector",
                                  (lambda g_, qb_, hx: lambda e:
                                   e.tensor_copy(
                                       kpair[hx * 64:(hx + 1) * 64, pr,
                                             g_ * 512:(g_ + 1) * 512],
                                       bankap(qb_, 512, 64, 64)))(
                                       g, qb2, hix),
                                  inc="vv")
                        qk_bank_free[qb2] = ("vv", vvk)
                        vv_qk_pair[pi] = vvk
                    chunks.append(chunk)
            return chunks

        for npair in range(NH // 2):
            heads = (2 * npair, 2 * npair + 1)
            par = npair % 2
            if npair == 0:
                for ch in make_qk_chunks(0, load_pair_slots(heads)):
                    ch()
            vv_vh = 0
            for hix, n in enumerate(heads):
                vdrain()
                vv_vh = add("vector", (lambda n_, hx: lambda e: e.tensor_copy(
                    AP(vaug_h, hx * TC * 128, [[2 * TC * 128, 128],
                                               [128, TC], [1, 64]]),
                    AP(v_all, 64 * n_, [[TC * 1024, 128],
                                        [1024, TC], [1, 64]])))(n, hix),
                    waits=[vv_vaug_done, ("vv", vv_vones),
                           vaug_free[hix]],
                    inc="vv")
            if npair + 1 < NH // 2:
                next_chunks = make_qk_chunks(
                    npair + 1,
                    load_pair_slots((2 * npair + 2, 2 * npair + 3)))
            else:
                next_chunks = []
            vv_qk_done = vv_qk_pair[npair]
            if pending_permute is not None:
                perm_done = pending_permute()
                pending_permute = None

            steps = [(qb, j) for qb in range(4) for j in range(4 * qb + 4)]
            ex_exp = {}
            mm_pv = {}
            reg_of = {}

            def emit_se(si):
                nonlocal ex_attn_done
                qb, j = steps[si]
                qlo = max(128 * j, 512 * qb)
                qhi = 512 * (qb + 1)
                w = qhi - qlo
                off = qlo - 512 * qb
                reg = reg_of[si]
                base = REG_BASE[reg]
                reg_prev = s_reg_free[reg]
                ppar_guard = ([("mm", mm_pv[si - 3])] if si >= 3 else
                              ([("mm", p_guard[si % 3])]
                               if p_guard[si % 3] else []))
                mm_sc = None
                for hix in range(2):
                    w0 = [reg_prev] if hix == 0 else []
                    if si == 0 and hix == 0:
                        w0.append(("vv", vv_qk_done))
                    mm_sc = add("tensor",
                                (lambda j_, q_, w_, b_, hx, off_, p_:
                                 lambda e:
                                 e.matmul(
                                     AP(arena, b_ + hx * 512 + off_,
                                        [[AR, 128], [1, w_]]),
                                     kpair[hx * 64:(hx + 1) * 64, p_,
                                           j_ * 128:(j_ + 1) * 128],
                                     qpair[hx * 64:(hx + 1) * 64, p_,
                                           q_:q_ + w_],
                                     start=True, stop=True,
                                     skip_group_check=True))(
                                     j, qlo, w, base, hix, off, par),
                                waits=w0, inc="mm" if hix == 1 else None)
                if qlo == 128 * j:
                    for hix in range(2):
                        mm_sc = add("tensor",
                                    (lambda b_, hx, off_: lambda e:
                                     e.matmul(
                                         AP(arena, b_ + hx * 512 + off_,
                                            [[AR, 128], [1, 128]]),
                                         maskd[:, :], identb[:, :],
                                         start=False, stop=True,
                                         skip_group_check=True))(
                                         base, hix, off),
                                    waits=([("dx", dx_consts)]
                                           if hix == 0 else []),
                                    inc="mm" if hix == 1 else None)
                ew = [("mm", mm_sc)] + ppar_guard
                last_ex = add("scalar",
                              (lambda si_, w_, b_, off_: lambda e:
                               e.activation(
                                   AP(p_sb, (si_ % 3) * 1024,
                                      [[3 * 1024, 128], [512, 2], [1, w_]]),
                                   AP(arena, b_ + off_,
                                      [[AR, 128], [512, 2], [1, w_]]),
                                   AF.Exp, scale=cst_sb[:, 1:2]))(
                                   si, w, base, off),
                              waits=ew, inc="ex")
                s_reg_free[reg] = ("ex", last_ex)
                ex_exp[si] = last_ex
                ex_attn_done = last_ex

            def emit_pv(si):
                qb, j = steps[si]
                qlo = max(128 * j, 512 * qb)
                w = 512 * (qb + 1) - qlo
                off = qlo - 512 * qb
                jstop = 4 * qb + 3
                last = None
                for hix in range(2):
                    w0 = [("ex", ex_exp[si])] if hix == 0 else []
                    if j == 0 and hix == 0:
                        w0.append(o_bank_free[0])
                        w0.append(o_bank_free[1])
                        w0.append(("vv", vv_vh))
                    last = add("tensor",
                               (lambda j_, w_, hx, st_, si_, off_: lambda e:
                                e.matmul(
                                    AP(arena, (4 + hx) * 512 + off_,
                                       [[AR, 128], [1, w_]]),
                                    vaug_h[:, hx, j_ * 128:(j_ + 1) * 128],
                                    AP(p_sb, (si_ % 3) * 1024 + hx * 512,
                                       [[3 * 1024, 128], [1, w_]]),
                                    start=(j_ == 0), stop=st_,
                                    skip_group_check=True))(
                                    j, w, hix, j == jstop, si, off),
                               waits=w0, inc="mm" if hix == 1 else None)
                mm_pv[si] = last
                return last

            def emit_epilogue(qb, mmw):
                for hix in range(2):
                    vdrain()
                    vv_oc = add("vector", (lambda hx: lambda e: e.tensor_copy(
                        AP(oscr_sb, hx * 512, [[1024, 128], [1, 512]]),
                        AP(arena, (4 + hx) * 512, [[AR, 128], [1, 512]])))(hix),
                        waits=[("mm", mmw), oscr_free],
                        inc="vv")
                    o_bank_free[hix] = ("vv", vv_oc)
                vv_oc_last = o_bank_free[1]

                def epi_b(qb=qb, vv_oc_last=vv_oc_last, pd=perm_done):
                    nonlocal oscr_free
                    add("scalar", lambda e: e.activation(
                        rec_sb[0:64, :],
                        AP(oscr_sb, 64 * 1024, [[1024, 64], [1, 1024]]),
                        AF.Ln),
                        waits=[vv_oc_last, ("vv", pd)])
                    add("scalar", lambda e: e.drain())
                    ex_rec = add("scalar", lambda e: e.activation(
                        rec_sb[0:64, :], rec_sb[0:64, :],
                        AF.Exp, scale=cst_sb[0:64, 2:3]),
                        waits=[("vv", vv_cst)], inc="ex")
                    vdrain()
                    vv_on = None
                    for hix in range(2):
                        vv_on = add("vector", (lambda qb_, hx: lambda e:
                                    e.tensor_mul(
                                        AP(onorm_sb, hx * S + qb_ * 512,
                                           [[2 * S, 64], [1, 512]]),
                                        AP(oscr_sb, hx * 512,
                                           [[1024, 64], [1, 512]]),
                                        rec_sb[0:64,
                                               hx * 512:(hx + 1) * 512]))(
                                        qb, hix),
                                    waits=[("ex", ex_rec)],
                                    inc="vv")
                    oscr_free = ("vv", vv_on)
                return epi_b

            s_reg_free[2] = ("vv", vv_qk_pair[npair])
            rot = []
            ck_slots = {8, 17, 26, 35} if next_chunks else set()
            si_ctr = 0
            kk = 0
            rc = 0
            while si_ctr < len(steps):
                if rc in ck_slots and kk < len(next_chunks):
                    rot.append(("ck", kk))
                    kk += 2
                else:
                    reg_of[si_ctr] = rc % 3
                    rot.append(("se", si_ctr))
                    si_ctr += 1
                rc += 1
            r = 0
            nse = 0
            npv = 0
            epi_b_pend = None
            epi_b_due = 0
            while npv < len(steps):
                if r < len(rot) and nse - npv < 3:
                    kind, v = rot[r]
                    if kind == "se":
                        emit_se(v)
                        nse += 1
                    else:
                        for ck in (v, v + 1):
                            next_chunks[ck](extra=[s_reg_free[2]])
                        s_reg_free[2] = ("vv", vv_qk_pair[npair + 1])
                    r += 1
                else:
                    emit_pv(npv)
                    qb, j = steps[npv]
                    if j == 4 * qb + 3:
                        epi_b_pend = emit_epilogue(qb, mm_pv[npv])
                        epi_b_due = npv
                    npv += 1
                    if epi_b_pend is not None and npv >= epi_b_due:
                        epi_b_pend()
                        epi_b_pend = None
            if epi_b_pend is not None:
                epi_b_pend()
                epi_b_pend = None
            for si in range(len(steps) - 3, len(steps)):
                p_guard[si % 3] = mm_pv[si]
            vaug_free[0] = ("mm", C["mm"])
            vaug_free[1] = ("mm", C["mm"])

            def make_permutes(heads_):
                def emit():
                    nonlocal vv_attn_done
                    for hix, n_ in enumerate(heads_):
                        for h2 in range(2):
                            vdrain()
                            add("vector",
                                (lambda nn, h2_, hx: lambda e: e.tensor_mul(
                                    tmp_sb[h2_ * 64:(h2_ + 1) * 64, :, :],
                                    AP(onorm_sb, hx * S + h2_,
                                       [[2 * S, 64], [2, TC], [32, 64]]),
                                    AP(WB_sb, 64 * nn,
                                       [[D, 64], [0, TC], [1, 64]])))(
                                    n_, h2, hix),
                                waits=[("vv", vv_wb)])
                            vdrain()
                            vv_attn_done = add(
                                "vector",
                                (lambda nn, h2_: lambda e: e.tensor_add(
                                    x_sb[h2_ * 64:(h2_ + 1) * 64, :,
                                         64 * nn:64 * nn + 64],
                                    x_sb[h2_ * 64:(h2_ + 1) * 64, :,
                                         64 * nn:64 * nn + 64],
                                    tmp_sb[h2_ * 64:(h2_ + 1) * 64, :, :]))(
                                    n_, h2), inc="vv")
                    return vv_attn_done
                return emit
            pending_permute = make_permutes(heads)

        if pending_permute is not None:
            perm_done = pending_permute()
            pending_permute = None

        actx.close()
        h1_sb = sb("h1_sb", [128, NWW, TF], BF16)
        y_sb = sb("y_sb", [128, 4, D], F32)
        w2stage = sb("w2stage", [128, 8, 1024], BF16)

        f1_bank_free = [("vv", 0)] * 4 + [o_bank_free[0], o_bank_free[1]]
        ln2_rstd = [0, 0]
        ln2_apps = [None, None]
        ffn2_evac4 = ("vv", 0)
        w2q_free = [("mm", 0)] * 4
        out_bank_free = []
        ysb_free = [("dx", 0)] * 4
        dx_final = 0
        for g in range(NG):
            if g == 0:
                vst = 0
                for t in range(4):
                    vst = ln_stats(g * 4 + t, (g % 2) * 4 + t,
                                   extra_v=[("vv", vv_attn_done)])
                ln2_rstd[g % 2] = ln_rstd((g % 2) * 4, 4, vst)
                ln2_apps[g % 2] = [
                    ln_apply_only(g * 4 + t, (g % 2) * 4 + t, t,
                                  ln2_rstd[g % 2],
)
                    for t in range(4)]
            for t in range(4):
                c = g * 4 + t
                extra_tp = ([("ex", ex_attn_done)] if (g == 0 and t <= 1)
                            else ([ffn2_evac4] if (g > 0 and t <= 1) else []))
                ln_transpose(lnf_sb, t * 128, t, ln2_apps[g % 2][t],
                             extra_tp=extra_tp)
            vv_ln2 = lnf_state["ready"]
            for ww in range(NWW):
                if ww % 4 == 0:
                    w1ap = [[W1N, 128], [128 * W1N, DC], [1, 512]]
                    if g == 0:
                        sp, dwp = load_slot_quad(
                            AP(w1_d, 128 * ww, w1ap),
                            [[512, DC], [1, 512]])
                        dsb_c = add("scalar",
                                    (lambda s_, ww_, a_: lambda e:
                                     e.dma_start(
                                         AP(w1b_d, 128 * ww_, a_),
                                         AP(wstage, s_ * 1024,
                                            [[8 * 1024, 128], [512, DC],
                                             [1, 512]])))(sp, ww, w1ap),
                                    waits=[dwp],
                                    inc=f"dsb{(ww // 4) % 2}", amt=16)
                        wb_pend = (f"dsb{(ww // 4) % 2}", dsb_c)
                    else:
                        sp, dwp = load_slot_quad(
                            AP(w1b_d, 128 * ww, w1ap),
                            [[512, DC], [1, 512]],
                            eng="scalar", pre="dh")
                        wb_pend = None
                s, dww = sp, dwp
                wpar = ww % 4
                fb = ww % 6
                mm_f1 = None
                for dcc in range(DC):
                    w0 = ([dww, vv_ln2, f1_bank_free[fb]]
                          if dcc == 0 else [])
                    mm_f1 = add("tensor",
                                (lambda s_, dc_, fb_, wp_: lambda e: e.matmul(
                                    bankap(fb_, TF),
                                    AP(wstage, s_ * 1024 + dc_ * 512
                                       + wp_ * 128,
                                       [[8 * 1024, 128], [1, 128]]),
                                    lnf_sb[:, dc_, 0:TF],
                                    start=(dc_ == 0), stop=(dc_ == DC - 1),
                                    skip_group_check=True))(s, dcc, fb, wpar),
                                waits=w0,
                                inc="mm" if dcc == DC - 1 else None)
                if ww % 4 == 3:
                    for i in range(4):
                        slot_free[s + i] = ([("mm", mm_f1), wb_pend]
                                            if wb_pend else ("mm", mm_f1))
                vv_r = add("vector", (lambda ww_, fb_: lambda e: e.tensor_scalar(
                    h1_sb[:, ww_, :], bankap(fb_, TF),
                    b1c_sb[:, ww_:ww_ + 1], 0.0,
                    op0=ALU.add, op1=ALU.max))(ww, fb),
                    waits=[("mm", mm_f1)] +
                          ([("dbc", 16 * NWW)] if ww == 0 else []), inc="vv")
                f1_bank_free[fb] = ("vv", vv_r)
            vv_h1 = C["vv"]
            for ww in range(NWW):
                if ww % 2 == 0:
                    wq = (ww // 2) % 4
                    sp2 = wq * 2
                    add("gpsimd",
                        (lambda sp_, ww_: lambda e: e.dma_start(
                            AP(w2stage, sp_ * 1024,
                               [[8 * 1024, 128], [1024, 2], [1, 1024]]),
                            AP(w2_d, ww_ * 128 * D,
                               [[D, 128], [128 * D, 2], [1, D]])))(sp2, ww),
                        waits=[w2q_free[wq]], inc=f"dv{wq}", amt=16)
                    dwp = (f"dv{wq}", C[f"dv{wq}"])
                s, dww = sp2 + (ww % 2), dwp
                mm_t = None
                for t in range(8):
                    c2, dh = t // 2, t % 2
                    w0 = []
                    if t == 0:
                        w0 = [dww]
                        if ww == 0:
                            w0 += [("vv", vv_h1)] + out_bank_free
                    elif t == 7:
                        w0 = [(dww[0], dww[1], True)]
                    mm_t = add("tensor",
                               (lambda s_, ww_, c2_, dh_, t_: lambda e: e.matmul(
                                   bankap(t_, 512),
                                   h1_sb[:, ww_, 128 * c2_:128 * (c2_ + 1)],
                                   w2stage[:, s_, 512 * dh_:512 * (dh_ + 1)],
                                   start=(ww_ == 0), stop=False,
                                   skip_group_check=True))(s, ww, c2, dh, t),
                               waits=w0,
                               inc="mm" if t == 7 else None)
                if ww % 2 == 1:
                    w2q_free[(ww // 2) % 4] = ("mm", mm_t)
            mm_aug = [None] * 8
            for t in range(8):
                dh = t % 2
                mm_aug[t] = add("tensor",
                                (lambda dh_, t_: lambda e: e.matmul(
                                    bankap(t_, 512),
                                    ones_sb[0:1, 0:128],
                                    b2r_sb[0:1, 512 * dh_:512 * (dh_ + 1)],
                                    start=False, stop=True,
                                    skip_group_check=True))(dh, t),
                                waits=[("dwb", 32)], inc="mm")
            if g + 1 < NG:
                vst = 0
                for t in range(4):
                    vst = ln_stats((g + 1) * 4 + t, ((g + 1) % 2) * 4 + t)
                ln2_rstd[(g + 1) % 2] = ln_rstd(((g + 1) % 2) * 4, 4, vst)
                ln2_apps[(g + 1) % 2] = [
                    ln_apply_only((g + 1) * 4 + t, ((g + 1) % 2) * 4 + t,
                                  t, ln2_rstd[(g + 1) % 2],
)
                    for t in range(4)]
            out_bank_free = []
            for t in range(8):
                c2, dh = t // 2, t % 2
                cglob = g * 4 + c2
                yp = cglob % 4
                vv_y = add("vector",
                           (lambda t_, dh_, cg, yp_: lambda e: e.tensor_add(
                               y_sb[:, yp_, 512 * dh_:512 * (dh_ + 1)],
                               bankap(t_, 512),
                               x_sb[:, cg, 512 * dh_:512 * (dh_ + 1)]))(
                               t, dh, cglob, yp),
                           waits=[("mm", mm_aug[t])] +
                                 ([ysb_free[yp]] if dh == 0 else []),
                           inc="vv")
                out_bank_free.append(("vv", vv_y))
                if t == 3:
                    ffn2_evac4 = ("vv", vv_y)
                if dh == 1:
                    dyc = add("sync",
                              (lambda cg, yp_: lambda e: e.dma_start(
                                  AP(y_d, cg * 128 * D, [[D, 128], [1, D]]),
                                  y_sb[:, yp_, :]))(cglob, yp),
                              waits=[("vv", vv_y)], inc=f"dy{yp}", amt=16)
                    ysb_free[yp] = (f"dy{yp}", dyc)
            for b in range(6):
                f1_bank_free[b] = out_bank_free[b]

        add("sync", None, waits=[("dx", C["dx"]), ("dbc", C["dbc"]),
                                 ("dsb0", C["dsb0"]), ("dsb1", C["dsb1"])] +
                          [(f"dy{i}", C[f"dy{i}"]) for i in range(4)] +
                          [(f"dq{i}", C[f"dq{i}"]) for i in range(4)] +
                          [(f"dh{i}", C[f"dh{i}"]) for i in range(8)])
        add("gpsimd", None,
            waits=[("dwb", C["dwb"])] +
                  [(f"dv{i}", C[f"dv{i}"]) for i in range(4)] +
                  [(f"dw{i}", C[f"dw{i}"]) for i in range(8)])

        sems = {name: ctx.enter_context(nc.semaphore(name)) for name in C}

        with nc.Block() as block:
            def emit_for(eng_name):
                def body(eng_obj):
                    last_wait = {}
                    for (e, waits, emit, inc, amt) in seq:
                        if e != eng_name:
                            continue
                        for wt in waits:
                            s_, v = wt[0], wt[1]
                            force = len(wt) > 2 and wt[2]
                            if v > 0 and (force or last_wait.get(s_, -1) < v):
                                eng_obj.wait_ge(sems[s_], v)
                                last_wait[s_] = max(last_wait.get(s_, -1), v)
                        if emit is None:
                            continue
                        inst = emit(eng_obj)
                        if inc is not None:
                            inst.then_inc(sems[inc], amt)
                return body

            block.sync(emit_for("sync"))
            block.gpsimd(emit_for("gpsimd"))
            block.tensor(emit_for("tensor"))
            block.vector(emit_for("vector"))
            block.scalar(emit_for("scalar"))

    return nc


def kernel(L0_attn_x, L0_attn_W_qkv, L0_attn_W_out, L0_ffn_w1, L0_ffn_b1,
           L0_ffn_w2, L0_ffn_b2):
    x = np.ascontiguousarray(np.asarray(L0_attn_x, dtype=np.float32))
    wqkv = np.ascontiguousarray(np.asarray(L0_attn_W_qkv, dtype=np.float32))
    wout = np.ascontiguousarray(np.asarray(L0_attn_W_out, dtype=np.float32))
    w1 = np.ascontiguousarray(np.asarray(L0_ffn_w1, dtype=np.float32))
    b1 = np.ascontiguousarray(np.asarray(L0_ffn_b1, dtype=np.float32)).reshape(1, -1)
    w2 = np.ascontiguousarray(np.asarray(L0_ffn_w2, dtype=np.float32))
    b2 = np.ascontiguousarray(np.asarray(L0_ffn_b2, dtype=np.float32)).reshape(1, -1)

    identf = np.eye(128, dtype=np.float32)
    maskd = np.triu(np.full((128, 128), NEG, dtype=np.float32), 1)

    nc = build_program()
    in_maps = []
    for b in range(8):
        in_maps.append({
            "x": np.ascontiguousarray(x[b]),
            "wqkv": wqkv, "wout": wout, "w1": w1, "b1": b1,
            "w2": w2, "b2": b2, "identf": identf, "maskd": maskd,
        })
    res = run_bass_kernel_spmd(nc, in_maps, core_ids=list(range(8)))
    out = np.stack([np.asarray(r["y"]) for r in res.results], axis=0)
    return out
